# version 19
# baseline (speedup 1.0000x reference)
import numpy as np
import ml_dtypes
import concourse.bass as bass
import concourse.mybir as mybir
from concourse.bass_utils import run_bass_kernel_spmd
from contextlib import ExitStack

F32 = mybir.dt.float32
BF16 = mybir.dt.bfloat16
I8 = mybir.dt.int8
U8 = mybir.dt.uint8
AF = mybir.ActivationFunctionType
ALU = mybir.AluOpType
AX = mybir.AxisListType

B = 4
CTX = 4096
TOK = 2048
DM = 2048
H = 16
DH = 128
INTER = 8192
SC = float(1.0 / np.sqrt(DH))
EXP_BIAS = -4.0
EPS = 1e-5

NT_CTX = CTX // 128
NC_CTX = CTX // 512
NC_OWN = TOK // 512
NT_OWN = TOK // 128

ENGS = ["sync", "gpsimd", "scalar", "vector", "tensor"]


class DS:

    def __init__(self, nc, e, name, n=2, per=1):
        self.n = n
        self.per = per
        self.sems = [e(nc.semaphore(f"{name}{i}")) for i in range(n)]

    def inc(self, dma, step):
        dma.then_inc(self.sems[step % self.n], 16)

    def wait(self, eng, step):
        eng.wait_ge(self.sems[step % self.n], 16 * self.per * (step // self.n + 1))

    def drain(self, eng, nsteps):
        for p in range(self.n):
            cnt = (nsteps - 1 - p) // self.n + 1 if nsteps > p else 0
            if cnt > 0:
                eng.wait_ge(self.sems[p], 16 * self.per * cnt)


def build_nc(debug_scratch=False):
    nc = bass.Bass("TRN2", target_bir_lowering=False)
    SCR = "ExternalOutput" if debug_scratch else "Internal"

    x_ctx = nc.dram_tensor("x_ctx", [CTX, DM], F32, kind="ExternalInput").ap()
    wq_t = nc.dram_tensor("wq_t", [DM, DM], F32, kind="ExternalInput").ap()
    wkv_t = nc.dram_tensor("wkv_t", [DM, 2 * DM], F32, kind="ExternalInput").ap()
    wout_t = nc.dram_tensor("wout_t", [DM, DM], F32, kind="ExternalInput").ap()
    w_gate = nc.dram_tensor("w_gate", [DM, INTER], I8, kind="ExternalInput").ap()
    w_up = nc.dram_tensor("w_up", [DM, INTER], I8, kind="ExternalInput").ap()
    w_down = nc.dram_tensor("w_down", [INTER, DM], I8, kind="ExternalInput").ap()
    sg_pt = nc.dram_tensor("sg_pt", [128, INTER // 128], F32, kind="ExternalInput").ap()
    su_pt = nc.dram_tensor("su_pt", [128, INTER // 128], F32, kind="ExternalInput").ap()
    s_down = nc.dram_tensor("s_down", [DM], F32, kind="ExternalInput").ap()
    cosq = nc.dram_tensor("cosq", [128, TOK], F32, kind="ExternalInput").ap()
    sinq = nc.dram_tensor("sinq", [128, TOK], F32, kind="ExternalInput").ap()
    cosk = nc.dram_tensor("cosk", [128, CTX], F32, kind="ExternalInput").ap()
    sink = nc.dram_tensor("sink", [128, CTX], F32, kind="ExternalInput").ap()
    ident = nc.dram_tensor("ident", [128, 128], BF16, kind="ExternalInput").ap()

    y_out = nc.dram_tensor("y_out", [TOK, DM], F32, kind="ExternalOutput").ap()

    xnt_d = nc.dram_tensor("xnt_d", [16, 128, CTX], BF16, kind=SCR).ap()
    qt_d = nc.dram_tensor("qt_d", [H, 128, TOK], BF16, kind=SCR).ap()
    kt_d = nc.dram_tensor("kt_d", [H, 128, CTX], BF16, kind=SCR).ap()
    v_d = nc.dram_tensor("v_d", [CTX, DM], BF16, kind=SCR).ap()
    ayx_d = nc.dram_tensor("ayx_d", [TOK, DM], F32, kind=SCR).ap()
    fuse_d = nc.dram_tensor("fuse_d", [64, 128, TOK], BF16, kind=SCR).ap()
    rcp_d = nc.dram_tensor("rcp_d", [4, 512], F32, kind="Internal").ap()

    with ExitStack() as stack:
        e = stack.enter_context

        aw_g = e(nc.sbuf_tensor("arena_w", [128, 131072], U8))
        AW = aw_g.ap() if hasattr(aw_g, "ap") else aw_g[:]
        as_g = e(nc.sbuf_tensor("arena_s", [128, 65536], U8))
        AS = as_g.ap() if hasattr(as_g, "ap") else as_g[:]

        def cw(off, nbytes, dt):
            return AW[:, off:off + nbytes].bitcast(dt)

        def cs(off, nbytes, dt):
            return AS[:, off:off + nbytes].bitcast(dt)

        s_ident = e(nc.sbuf_tensor("s_ident", [128, 128], BF16))
        s_ones = e(nc.sbuf_tensor("s_ones", [128, 1], BF16))
        s_ebias = e(nc.sbuf_tensor("s_ebias", [128, 1], F32))
        s_eps = e(nc.sbuf_tensor("s_eps", [128, 1], F32))
        s_sum = e(nc.sbuf_tensor("s_sum", [128, 32], F32))
        s_ssq = e(nc.sbuf_tensor("s_ssq", [128, 32], F32))
        s_rstd = e(nc.sbuf_tensor("s_rstd", [128, 32], F32))
        s_nbias = e(nc.sbuf_tensor("s_nbias", [128, 32], F32))
        s_tmp32 = e(nc.sbuf_tensor("s_tmp32", [128, 32], F32))
        s_std32 = e(nc.sbuf_tensor("s_std32", [128, 32], F32))
        s_sgsu = e(nc.sbuf_tensor("s_sgsu", [128, 128], F32))
        s_rcp = e(nc.sbuf_tensor("s_rcp", [1, 1024], F32))
        p1_dummy_g = e(nc.sbuf_tensor("p1_dummy", [128, 2, 2048], BF16))
        _pd = p1_dummy_g.ap() if hasattr(p1_dummy_g, "ap") else p1_dummy_g[:]
        p1_dummy = [_pd[:, 0, :], _pd[:, 1, :]]

        p1_x4 = [cs(32768, 8192, F32), cs(40960, 8192, F32),
                 cs(49152, 8192, F32), cs(57344, 8192, F32)]
        p1_xn = [cs(16384, 4096, BF16), cs(20480, 4096, BF16)]
        p1_xnsb = [cs(24576, 4096, BF16), cs(28672, 4096, BF16)]


        wq_bf = [cw(dc * 4096, 4096, BF16) for dc in range(16)]
        p2_xnc = [cs(0, 16384, BF16).rearrange("p (c m) -> p c m", c=16),
                  cs(16384, 16384, BF16).rearrange("p (c m) -> p c m", c=16)]
        p2_cosq = cs(32768, 8192, F32)
        p2_sinq = cs(40960, 8192, F32)
        p2_tmp = cs(49152, 2048, F32)
        p2_tmp2 = cs(51200, 2048, F32)
        p2_qbf = [cs(53248, 1024, BF16), cs(54272, 1024, BF16)]

        wk_bf = [cw(65536 + dc * 4096, 4096, BF16) for dc in range(16)]
        wv_bf = [cw(dc * 4096, 4096, BF16) for dc in range(16)]
        p2b_cos = [cs(57344, 2048, F32), cs(59392, 2048, F32)]
        p2b_sin = [cs(61440, 2048, F32), cs(63488, 2048, F32)]
        p2b_kbf = [cs(40960, 1024, BF16), cs(41984, 1024, BF16)]
        p2b_vsb = [cs(43008, 4096, BF16), cs(47104, 4096, BF16)]
        p2b_tmp = cs(51200, 2048, F32)
        p2b_tmp2 = cs(53248, 2048, F32)

        p3_ot = [cw(h * 4096, 4096, BF16) for h in range(H)]
        p3_kt = [cs(0, 8192, BF16), cs(8192, 8192, BF16)]
        p3_v = [cs(16384, 8192, BF16).rearrange("p (c m) -> p c m", c=32),
                cs(24576, 8192, BF16).rearrange("p (c m) -> p c m", c=32)]
        p3_qt = [cs(32768, 4096, BF16), cs(36864, 4096, BF16)]
        p3_ae = [cs(40960, 2048, BF16), cs(43008, 2048, BF16)]
        p3_rcpB = [cs(45056, 2048, F32), cs(47104, 2048, F32)]

        wout_bf = [cw(65536 + oc * 4096, 4096, BF16) for oc in range(16)]
        p4_xot = [cs(0, 8192, F32), cs(8192, 8192, F32)]
        p4_ay = [cs(16384, 8192, F32), cs(24576, 8192, F32)]

        f1_xno = [cw(dc * 4096, 4096, BF16) for dc in range(16)]
        f1_wg = [cw(65536, 16384, BF16).rearrange("p (c m) -> p c m", c=16),
                 cw(65536 + 16384, 16384, BF16).rearrange("p (c m) -> p c m", c=16)]
        f1_wu = [cw(98304, 16384, BF16).rearrange("p (c m) -> p c m", c=16),
                 cw(98304 + 16384, 16384, BF16).rearrange("p (c m) -> p c m", c=16)]
        f1_silu = [cs(0, 2048, F32), cs(2048, 2048, F32), cs(4096, 2048, F32), cs(6144, 2048, F32)]
        f1_fasm = [cs(8192, 4096, BF16), cs(12288, 4096, BF16)]

        f2_wd = [cw(0, 65536, BF16).rearrange("p (c m) -> p c m", c=64),
                 cw(65536, 65536, BF16).rearrange("p (c m) -> p c m", c=64)]
        f2_wd8 = cs(47104, 16384, I8).rearrange("p (c m) -> p c m", c=32)
        f2_fc = [cs(0, 16384, BF16).rearrange("p (c m) -> p c m", c=64),
                 cs(16384, 16384, BF16).rearrange("p (c m) -> p c m", c=64)]
        f2_ayt = [cs(32768, 2048, F32), cs(34816, 2048, F32)]
        f2_ytmp = cs(36864, 2048, F32)
        f2_ysb = [cs(38912, 2048, F32), cs(40960, 2048, F32)]
        f2_sdB = [cs(43008, 2048, F32), cs(45056, 2048, F32)]

        ps_a = e(nc.psum_tensor("ps_a", [128, 1024], F32))
        ps_b = e(nc.psum_tensor("ps_b", [128, 1024], F32))
        ps_c = e(nc.psum_tensor("ps_c", [128, 1024], F32))
        ps_d = e(nc.psum_tensor("ps_d", [128, 512], F32))
        ps_e = e(nc.psum_tensor("ps_e", [128, 512], F32))

        p1_pT = [ps_a[:, 0:512], ps_a[:, 512:1024], ps_b[:, 0:512], ps_b[:, 512:1024]]
        p2_pq = [ps_a[:, 0:512], ps_a[:, 512:1024], ps_b[:, 0:512], ps_b[:, 512:1024]]
        p2_pv = [ps_c[:, 0:512], ps_c[:, 512:1024]]
        p3_pl = [ps_a, ps_b]
        p3_pS = [ps_d[0:1, 0:512], ps_e[0:1, 0:512]]
        p3_pO = [ps_c[:, 0:512], ps_c[:, 512:1024]]
        p4_po = [ps_a[:, 0:512], ps_a[:, 512:1024]]
        f1_pg = [ps_a[:, 0:512], ps_a[:, 512:1024], ps_c[:, 0:512], ps_c[:, 512:1024]]
        f1_pu = [ps_b[:, 0:512], ps_b[:, 512:1024], ps_d[:, :], ps_e[:, :]]
        f2_pd = [ps_a[:, 0:512], ps_a[:, 512:1024]]

        barv = {en: e(nc.semaphore(f"bar_{en}")) for en in ENGS}

        def bar_end(eng, en):
            eng.sem_inc(barv[en], 1)

        def bar_wait(eng, k):
            for en in ENGS:
                eng.wait_ge(barv[en], k)

        g_id = e(nc.semaphore("g_id"))
        q_wl = e(nc.semaphore("q_wl"))
        k_wl = e(nc.semaphore("k_wl"))
        v_wl = e(nc.semaphore("v_wl"))
        q_tb = e(nc.semaphore("q_tb"))
        wd8_l = e(nc.semaphore("wd8_l"))
        wd8_c = e(nc.semaphore("wd8_c"))
        kv_wl = e(nc.semaphore("kv_wl"))
        o_wl = e(nc.semaphore("o_wl"))
        f_pl = e(nc.semaphore("f_pl"))

        sp1_ld = DS(nc, e, "p1ld", n=4)
        sp1_sp = DS(nc, e, "p1sp")
        sq_ld = DS(nc, e, "qld")
        sq_sp = DS(nc, e, "qsp")
        skv_ld = DS(nc, e, "kvld", per=3)
        skv_sp = DS(nc, e, "kvsp")
        skv_vsp = DS(nc, e, "kvvsp")
        sa_hl = DS(nc, e, "ahl", per=3)
        sa_bs = DS(nc, e, "abs")
        sa_bb = DS(nc, e, "abb")
        so_xl = DS(nc, e, "oxl")
        so_sp = DS(nc, e, "osp")
        sf_wl = DS(nc, e, "fwl", per=2)
        sf_sp = DS(nc, e, "fsp")
        sd_wl = DS(nc, e, "dwl")
        sd_sl = DS(nc, e, "dsl")
        sd_fl = DS(nc, e, "dfl")
        sd_al = DS(nc, e, "dal")
        sd_st = DS(nc, e, "dst")

        p1_sq = e(nc.semaphore("p1_sq"))
        p1_rd = e(nc.semaphore("p1_rd"))
        p1_st = e(nc.semaphore("p1_st"))
        sch = e(nc.semaphore("sch"))
        p1_v1 = e(nc.semaphore("p1_v1"))
        p1_sd = e(nc.semaphore("p1_sd"))
        p1_v2 = e(nc.semaphore("p1_v2"))
        p1_xnr = e(nc.semaphore("p1_xnr"))
        p1_tt = e(nc.semaphore("p1_tt"))
        p1_ev = e(nc.semaphore("p1_ev"))
        q_mm = e(nc.semaphore("q_mm"))
        q_rp = e(nc.semaphore("q_rp"))
        kv_km = e(nc.semaphore("kv_km"))
        kv_rp = e(nc.semaphore("kv_rp"))
        kv_vm = e(nc.semaphore("kv_vm"))
        kv_ve = e(nc.semaphore("kv_ve"))
        a_lg = e(nc.semaphore("a_lg"))
        a_ex = e(nc.semaphore("a_ex"))
        a_vm = e(nc.semaphore("a_vm"))
        a_rc = e(nc.semaphore("a_rc"))
        a_od = e(nc.semaphore("a_od"))
        o_mm = e(nc.semaphore("o_mm"))
        o_ev = e(nc.semaphore("o_ev"))
        f_gm = e(nc.semaphore("f_gm"))
        f_mm = e(nc.semaphore("f_mm"))
        f_si = e(nc.semaphore("f_si"))
        f_fu = e(nc.semaphore("f_fu"))
        d_mm = e(nc.semaphore("d_mm"))
        d_ev = e(nc.semaphore("d_ev"))
        vch = e(nc.semaphore("vch"))

        block = e(nc.Block())

        @block.sync
        def _(sync):
            for tt in range(NT_CTX):
                if tt >= 4:
                    sync.wait_ge(p1_xnr, tt - 3)
                sp1_ld.inc(sync.dma_start(p1_x4[tt % 4], x_ctx[tt * 128:(tt + 1) * 128, :]), tt)
            bar_end(sync, "sync")
            bar_wait(sync, 1)
            sync.dma_start(p2_cosq, cosq).then_inc(q_tb, 16)
            sync.dma_start(p2_sinq, sinq).then_inc(q_tb, 16)
            for tc in range(NC_OWN):
                if tc >= 2:
                    sync.wait_ge(q_rp, (tc - 1) * 16)
                sq_ld.inc(sync.dma_start(
                    p2_xnc[tc % 2],
                    xnt_d[:, :, tc * 512:(tc + 1) * 512].transpose([1, 0, 2]),
                ), tc)
            bar_end(sync, "sync")
            for tc in range(NC_CTX):
                if tc == 0:
                    sync.wait_ge(q_mm, 48)
                elif tc == 1:
                    sync.wait_ge(q_mm, 64)
                else:
                    sync.wait_ge(kv_rp, (tc - 1) * 16)
                    sync.wait_ge(kv_vm, (tc - 1) * 16)
                skv_ld.inc(sync.dma_start(
                    p2_xnc[tc % 2],
                    xnt_d[:, :, tc * 512:(tc + 1) * 512].transpose([1, 0, 2]),
                ), tc)
                skv_ld.inc(sync.dma_start(p2b_cos[tc % 2], cosk[:, tc * 512:(tc + 1) * 512]), tc)
                skv_ld.inc(sync.dma_start(p2b_sin[tc % 2], sink[:, tc * 512:(tc + 1) * 512]), tc)
            bar_end(sync, "sync")
            sync.wait_ge(kv_km, 128)
            sync.wait_ge(kv_vm, 128)
            skv_sp.drain(sync, 128)
            skv_vsp.drain(sync, 32)
            sq_sp.drain(sync, 64)

            def p3_head_load(h):
                if h >= 2:
                    sync.wait_ge(a_od, 4 * (h - 1))
                sa_hl.inc(sync.dma_start(p3_kt[h % 2], kt_d[h]), h)
                sa_hl.inc(sync.dma_start(
                    p3_v[h % 2],
                    v_d[:, h * 128:(h + 1) * 128].rearrange("(c p) d -> p c d", p=128),
                ), h)
                sa_hl.inc(sync.dma_start(p3_qt[h % 2], qt_d[h]), h)

            p3_head_load(0)
            for h in range(H):
                if h + 1 < H:
                    p3_head_load(h + 1)
                for qc in range(4):
                    idx = h * 4 + qc
                    sync.wait_ge(a_rc, idx + 1)
                    sa_bs.inc(sync.dma_start(
                        rcp_d[idx % 4, :][None, :],
                        s_rcp[0:1, (idx % 2) * 512:(idx % 2) * 512 + 512],
                    ), idx)
                    sa_bs.wait(sync, idx)
                    sa_bb.inc(sync.dma_start(
                        p3_rcpB[idx % 2], rcp_d[idx % 4, :][None, :].partition_broadcast(128)
                    ), idx)
            bar_end(sync, "sync")
            sync.wait_ge(a_vm, 1024)
            sync.wait_ge(a_od, 64)
            for tt in range(NT_OWN):
                if tt >= 2:
                    sync.wait_ge(o_ev, 4 * (tt - 1))
                so_xl.inc(sync.dma_start(p4_xot[tt % 2], x_ctx[tt * 128:(tt + 1) * 128, :]), tt)
            sync.wait_ge(o_mm, 64)
            for dc in range(16):
                sync.dma_start(f1_xno[dc], xnt_d[dc, :, 0:TOK]).then_inc(f_pl, 16)
            sync.dma_start(s_sgsu[:, 0:64], sg_pt).then_inc(f_pl, 16)
            sync.dma_start(s_sgsu[:, 64:128], su_pt).then_inc(f_pl, 16)
            bar_end(sync, "sync")
            bar_wait(sync, 5)
            sync.dma_start(
                f2_wd8, w_down[0:4096, 0:512].rearrange("(c p) m -> p c m", p=128)
            ).then_inc(wd8_l, 16)
            bar_end(sync, "sync")
            sync.wait_ge(f_fu, 256)
            for p in range(2):
                sync.wait_ge(sf_sp.sems[p], 16 * 32)
                sync.wait_ge(so_sp.sems[p], 16 * 8)
            for dq in range(4):
                if dq >= 2:
                    sync.wait_ge(d_ev, 16 * (dq - 1))
                sd_sl.inc(sync.dma_start(
                    f2_sdB[dq % 2],
                    s_down[dq * 512:(dq + 1) * 512][None, :].partition_broadcast(128),
                ), dq)
                for tt in range(NT_OWN):
                    i = dq * 16 + tt
                    if i >= 2:
                        sync.wait_ge(d_mm, i - 1)
                        sync.wait_ge(d_ev, i - 1)
                    sd_fl.inc(sync.dma_start(
                        f2_fc[i % 2], fuse_d[:, :, tt * 128:(tt + 1) * 128].transpose([1, 0, 2])
                    ), i)
                    sd_al.inc(sync.dma_start(
                        f2_ayt[i % 2], ayx_d[tt * 128:(tt + 1) * 128, dq * 512:(dq + 1) * 512]
                    ), i)
                    if i >= 1:
                        j = i - 1
                        sync.wait_ge(d_ev, j + 1)
                        sd_st.inc(sync.dma_start(
                            y_out[(j % 16) * 128:(j % 16 + 1) * 128,
                                  (j // 16) * 512:(j // 16 + 1) * 512], f2_ysb[j % 2]
                        ), j)
            sync.wait_ge(d_ev, 64)
            sd_st.inc(sync.dma_start(
                y_out[15 * 128:16 * 128, 3 * 512:4 * 512], f2_ysb[63 % 2]
            ), 63)
            sd_st.drain(sync, 64)
            bar_end(sync, "sync")

        @block.gpsimd
        def _(gpsimd):
            gpsimd.dma_start(s_ident[:], ident).then_inc(g_id, 16)
            for dc in range(16):
                gpsimd.dma_start(wq_bf[dc], wq_t[dc * 128:(dc + 1) * 128, :]).then_inc(q_wl, 16)
            for dc in range(16):
                gpsimd.dma_start(wk_bf[dc], wkv_t[dc * 128:(dc + 1) * 128, 0:2048]).then_inc(k_wl, 16)
            bar_end(gpsimd, "gpsimd")
            bar_wait(gpsimd, 1)
            gpsimd.wait_ge(q_mm, 64)
            for dc in range(16):
                gpsimd.dma_start(wv_bf[dc], wkv_t[dc * 128:(dc + 1) * 128, 2048:4096]).then_inc(v_wl, 16)
            bar_end(gpsimd, "gpsimd")
            bar_wait(gpsimd, 2)
            gpsimd.wait_ge(kv_km, 128)
            for oc in range(16):
                gpsimd.dma_start(wout_bf[oc], wout_t[oc * 128:(oc + 1) * 128, :]).then_inc(o_wl, 16)
            bar_end(gpsimd, "gpsimd")
            bar_wait(gpsimd, 3)
            bar_end(gpsimd, "gpsimd")
            bar_wait(gpsimd, 4)
            gpsimd.wait_ge(o_mm, 64)
            for itb in range(2):
                sf_wl.inc(gpsimd.dma_start(
                    f1_wg[itb % 2], w_gate[:, itb * 512:(itb + 1) * 512].rearrange("(c p) m -> p c m", p=128)
                ), itb)
                sf_wl.inc(gpsimd.dma_start(
                    f1_wu[itb % 2], w_up[:, itb * 512:(itb + 1) * 512].rearrange("(c p) m -> p c m", p=128)
                ), itb)
            bar_end(gpsimd, "gpsimd")
            bar_wait(gpsimd, 5)
            for itb in range(2, 16):
                gpsimd.wait_ge(f_mm, 16 * (itb - 1))
                sf_wl.inc(gpsimd.dma_start(
                    f1_wg[itb % 2], w_gate[:, itb * 512:(itb + 1) * 512].rearrange("(c p) m -> p c m", p=128)
                ), itb)
                sf_wl.inc(gpsimd.dma_start(
                    f1_wu[itb % 2], w_up[:, itb * 512:(itb + 1) * 512].rearrange("(c p) m -> p c m", p=128)
                ), itb)
            gpsimd.wait_ge(f_mm, 256)
            gpsimd.wait_ge(wd8_l, 16)
            sd_wl.inc(gpsimd.dma_start(
                f2_wd[0][:, 32:64, :],
                w_down[4096:8192, 0:512].rearrange("(c p) m -> p c m", p=128)
            ), 0)
            gpsimd.tensor_copy(
                f2_wd[0][:, 12:22, :].rearrange("p c m -> p (c m)"),
                f2_wd8[:, 12:22, :].rearrange("p c m -> p (c m)"),
            ).then_inc(wd8_c, 1)
            sd_wl.inc(gpsimd.dma_start(
                f2_wd[1], w_down[:, 512:1024].rearrange("(c p) m -> p c m", p=128)
            ), 1)
            bar_end(gpsimd, "gpsimd")
            bar_wait(gpsimd, 6)
            for dq in range(2, 4):
                gpsimd.wait_ge(d_mm, 16 * (dq - 1))
                sd_wl.inc(gpsimd.dma_start(
                    f2_wd[dq % 2], w_down[:, dq * 512:(dq + 1) * 512].rearrange("(c p) m -> p c m", p=128)
                ), dq)
            bar_end(gpsimd, "gpsimd")

        @block.scalar
        def _(scalar):
            def p1_sq_emit(tt):
                sp1_ld.wait(scalar, tt)
                if tt >= 2:
                    scalar.wait_ge(p1_sq, tt - 1)
                scalar.activation(
                    p1_dummy[tt % 2], p1_x4[tt % 4], AF.Square,
                    accum_out=s_ssq[:, tt:tt + 1],
                ).then_inc(p1_sq, 1)

            def p1_sqrt_emit(tt):
                scalar.wait_ge(p1_st, tt + 1)
                scalar.activation(
                    s_std32[:, tt:tt + 1], s_tmp32[:, tt:tt + 1], AF.Sqrt, bias=s_eps[:]
                ).then_inc(p1_sd, 1)

            def p1_norm_emit(tt):
                scalar.wait_ge(p1_v2, tt + 1)
                if tt >= 2:
                    scalar.wait_ge(p1_tt, tt - 1)
                scalar.activation(
                    p1_xn[tt % 2], p1_x4[tt % 4], AF.Identity,
                    scale=s_rstd[:, tt:tt + 1], bias=s_nbias[:, tt:tt + 1],
                ).then_inc(p1_xnr, 1)

            def p1_spill_emit(tt):
                scalar.wait_ge(p1_ev, 4 * (tt + 1))
                sp1_sp.inc(scalar.dma_start(
                    xnt_d[:, :, tt * 128:(tt + 1) * 128].transpose([1, 0, 2]),
                    p1_xnsb[tt % 2].rearrange("p (c m) -> p c m", c=16),
                ), tt)

            p1_sq_emit(0)
            p1_sq_emit(1)
            for tt in range(NT_CTX):
                if tt + 2 < NT_CTX:
                    p1_sq_emit(tt + 2)
                p1_sqrt_emit(tt)
                if tt >= 1:
                    p1_norm_emit(tt - 1)
                if tt >= 3:
                    p1_spill_emit(tt - 3)
            p1_norm_emit(NT_CTX - 1)
            p1_spill_emit(NT_CTX - 3)
            p1_spill_emit(NT_CTX - 2)
            p1_spill_emit(NT_CTX - 1)
            sp1_sp.drain(scalar, NT_CTX)
            bar_end(scalar, "scalar")

            bar_wait(scalar, 1)
            for tc in range(NC_OWN):
                for qt in range(16):
                    i = tc * 16 + qt
                    scalar.wait_ge(q_rp, i + 1)
                    sq_sp.inc(scalar.dma_start(
                        qt_d[qt][:, tc * 512:(tc + 1) * 512], p2_qbf[i % 2]
                    ), i)
            sq_sp.drain(scalar, 64)
            bar_end(scalar, "scalar")

            for tc in range(NC_CTX):
                for kt in range(16):
                    i = tc * 16 + kt
                    scalar.wait_ge(kv_rp, i + 1)
                    skv_sp.inc(scalar.dma_start(
                        kt_d[kt][:, tc * 512:(tc + 1) * 512], p2b_kbf[i % 2]
                    ), i)
                for vt in range(4):
                    j = tc * 4 + vt
                    scalar.wait_ge(kv_ve, 4 * (j + 1))
                    skv_vsp.inc(scalar.dma_start(
                        v_d[j * 128:(j + 1) * 128, :], p2b_vsb[j % 2]
                    ), j)
            skv_sp.drain(scalar, 128)
            skv_vsp.drain(scalar, 32)
            bar_end(scalar, "scalar")

            for h in range(H):
                for qc in range(4):
                    idx = h * 4 + qc
                    for pr in range(16):
                        p = idx * 16 + pr
                        scalar.wait_ge(a_lg, p + 1)
                        if p >= 2:
                            scalar.wait_ge(a_vm, p - 1)
                        scalar.activation(
                            p3_ae[pr % 2], p3_pl[pr % 2][:], AF.Exp,
                            bias=s_ebias[:], scale=SC,
                        ).then_inc(a_ex, 1)
            bar_end(scalar, "scalar")

            for tt in range(NT_OWN):
                scalar.wait_ge(o_ev, 4 * (tt + 1))
                so_sp.inc(scalar.dma_start(
                    ayx_d[tt * 128:(tt + 1) * 128, :], p4_ay[tt % 2]
                ), tt)
            so_sp.drain(scalar, NT_OWN)
            bar_end(scalar, "scalar")

            bar_wait(scalar, 5)
            for itb in range(16):
                for it in range(4):
                    git = itb * 4 + it
                    if git >= 1:
                        scalar.wait_ge(f_fu, 4 * git)
                        sf_sp.inc(scalar.dma_start(fuse_d[git - 1], f1_fasm[(git - 1) % 2]), git - 1)
                    for tc in range(4):
                        i = git * 4 + tc
                        scalar.wait_ge(f_gm, i + 1)
                        if i >= 4:
                            scalar.wait_ge(f_fu, i - 3)
                        scalar.activation(
                            f1_silu[i % 4], f1_pg[i % 4], AF.Silu,
                            scale=s_sgsu[:, itb * 4 + it: itb * 4 + it + 1],
                        ).then_inc(f_si, 1)
            scalar.wait_ge(f_fu, 4 * 64)
            sf_sp.inc(scalar.dma_start(fuse_d[63], f1_fasm[63 % 2]), 63)
            sf_sp.drain(scalar, 64)
            bar_end(scalar, "scalar")

            bar_wait(scalar, 6)
            scalar.wait_ge(wd8_l, 16)
            scalar.copy(
                f2_wd[0][:, 22:32, :].rearrange("p c m -> p (c m)"),
                f2_wd8[:, 22:32, :].rearrange("p c m -> p (c m)"),
            ).then_inc(wd8_c, 1)
            bar_end(scalar, "scalar")

        @block.vector
        def _(vector):
            _vch = [0]

            def chain(ins, n=1):
                ins.then_inc(vch, 1)
                _vch[0] += 1
                vector.wait_ge(vch, _vch[0] + (n - 1) * 0)

            vector.memset(s_ones[:], 1.0).then_inc(vch, 1)
            vector.memset(s_ebias[:], float(EXP_BIAS)).then_inc(vch, 1)
            vector.memset(s_eps[:], float(EPS)).then_inc(vch, 1)
            _vch[0] += 3
            vector.wait_ge(vch, _vch[0])

            def p1_reduce_emit(tt):
                sp1_ld.wait(vector, tt)
                vector.reduce_sum(s_sum[:, tt:tt + 1], p1_x4[tt % 4], axis=AX.X).then_inc(p1_rd, 1)

            def p1_statsA_emit(tt):
                c = slice(tt, tt + 1)
                vector.wait_ge(p1_rd, tt + 1)
                vector.wait_ge(p1_sq, tt + 1)
                chain(vector.tensor_mul(s_tmp32[:, c], s_sum[:, c], s_sum[:, c]))
                chain(vector.tensor_scalar_mul(s_tmp32[:, c], s_tmp32[:, c], 1.0 / (DM * DM)))
                vector.scalar_tensor_tensor(
                    s_tmp32[:, c], s_ssq[:, c], 1.0 / DM, s_tmp32[:, c], ALU.mult, ALU.subtract
                ).then_inc(p1_st, 1)

            def p1_statsB_emit(tt):
                c = slice(tt, tt + 1)
                vector.wait_ge(p1_sd, tt + 1)
                chain(vector.reciprocal(s_rstd[:, c], s_std32[:, c]))
                chain(vector.tensor_mul(s_nbias[:, c], s_sum[:, c], s_rstd[:, c]))
                vector.tensor_scalar_mul(s_nbias[:, c], s_nbias[:, c], -1.0 / DM).then_inc(p1_v2, 1)

            def p1_evac_emit(tt):
                par = tt % 2
                vector.wait_ge(p1_tt, tt + 1)
                if tt >= 2:
                    sp1_sp.wait(vector, tt - 2)
                for k in range(4):
                    vector.tensor_copy(
                        p1_xnsb[par][:, k * 512:(k + 1) * 512], p1_pT[k]
                    ).then_inc(p1_ev, 1)

            p1_reduce_emit(0)
            for tt in range(NT_CTX):
                p1_statsA_emit(tt)
                if tt >= 1:
                    p1_statsB_emit(tt - 1)
                if tt + 1 < NT_CTX:
                    p1_reduce_emit(tt + 1)
                if tt >= 3:
                    p1_evac_emit(tt - 3)
            p1_statsB_emit(NT_CTX - 1)
            p1_evac_emit(NT_CTX - 3)
            p1_evac_emit(NT_CTX - 2)
            p1_evac_emit(NT_CTX - 1)
            bar_end(vector, "vector")

            bar_wait(vector, 1)
            for tc in range(NC_OWN):
                csl = slice(tc * 512, (tc + 1) * 512)
                for qt in range(16):
                    i = tc * 16 + qt
                    vector.wait_ge(q_mm, i + 1)
                    if i >= 1:
                        vector.wait_ge(q_rp, i)
                    if i >= 2:
                        sq_sp.wait(vector, i - 2)
                    pq = p2_pq[i % 4]
                    vector.tensor_mul(p2_tmp[0:64, :], pq[64:128, :], p2_sinq[0:64, csl]).then_inc(vch, 1)
                    vector.tensor_mul(p2_tmp[64:128, :], pq[0:64, :], p2_sinq[64:128, csl]).then_inc(vch, 1)
                    vector.tensor_mul(p2_tmp2[:], pq[:], p2_cosq[:, csl]).then_inc(vch, 1)
                    _vch[0] += 3
                    vector.wait_ge(vch, _vch[0])
                    vector.tensor_add(p2_qbf[i % 2], p2_tmp[:], p2_tmp2[:]).then_inc(q_rp, 1)
            bar_end(vector, "vector")

            sq_sp.drain(vector, 64)
            for tc in range(NC_CTX):
                for kt in range(16):
                    i = tc * 16 + kt
                    vector.wait_ge(kv_km, i + 1)
                    if i >= 1:
                        vector.wait_ge(kv_rp, i)
                    if i >= 2:
                        skv_sp.wait(vector, i - 2)
                    pk = p2_pq[i % 4]
                    vector.tensor_mul(p2b_tmp[0:64, :], pk[64:128, :], p2b_sin[tc % 2][0:64, :]).then_inc(vch, 1)
                    vector.tensor_mul(p2b_tmp[64:128, :], pk[0:64, :], p2b_sin[tc % 2][64:128, :]).then_inc(vch, 1)
                    vector.tensor_mul(p2b_tmp2[:], pk[:], p2b_cos[tc % 2]).then_inc(vch, 1)
                    _vch[0] += 3
                    vector.wait_ge(vch, _vch[0])
                    vector.tensor_add(p2b_kbf[i % 2], p2b_tmp[:], p2b_tmp2[:]).then_inc(kv_rp, 1)
                for vt in range(4):
                    j = tc * 4 + vt
                    if j >= 2:
                        skv_vsp.wait(vector, j - 2)
                    for vc in range(4):
                        jj = j * 4 + vc
                        vector.wait_ge(kv_vm, jj + 1)
                        vector.tensor_copy(
                            p2b_vsb[j % 2][:, vc * 512:(vc + 1) * 512], p2_pv[jj % 2]
                        ).then_inc(kv_ve, 1)
            bar_end(vector, "vector")

            for h in range(H):
                for qc in range(4):
                    idx = h * 4 + qc
                    vector.wait_ge(a_vm, 16 * (idx + 1))
                    if idx >= 2:
                        sa_bs.wait(vector, idx - 2)
                    vector.reciprocal(
                        s_rcp[0:1, (idx % 2) * 512:(idx % 2) * 512 + 512], p3_pS[idx % 2]
                    ).then_inc(a_rc, 1)
                    sa_bb.wait(vector, idx)
                    vector.tensor_mul(
                        p3_ot[h][:, qc * 512:(qc + 1) * 512], p3_pO[idx % 2], p3_rcpB[idx % 2]
                    ).then_inc(a_od, 1)
            bar_end(vector, "vector")

            vector.wait_ge(a_vm, 1024)
            for tt in range(NT_OWN):
                for dc in range(4):
                    i = tt * 4 + dc
                    vector.wait_ge(o_mm, i + 1)
                    so_xl.wait(vector, tt)
                    if tt >= 2:
                        so_sp.wait(vector, tt - 2)
                    vector.tensor_add(
                        p4_ay[tt % 2][:, dc * 512:(dc + 1) * 512], p4_po[i % 2],
                        p4_xot[tt % 2][:, dc * 512:(dc + 1) * 512],
                    ).then_inc(o_ev, 1)
            bar_end(vector, "vector")

            bar_wait(vector, 5)
            for itb in range(16):
                for it in range(4):
                    git = itb * 4 + it
                    if git >= 2:
                        sf_sp.wait(vector, git - 2)
                    for tc in range(4):
                        i = git * 4 + tc
                        vector.wait_ge(f_si, i + 1)
                        vector.wait_ge(f_mm, i + 1)
                        vector.scalar_tensor_tensor(
                            f1_fasm[git % 2][:, tc * 512:(tc + 1) * 512],
                            f1_pu[i % 4],
                            s_sgsu[:, 64 + itb * 4 + it: 64 + itb * 4 + it + 1],
                            f1_silu[i % 4],
                            ALU.mult, ALU.mult,
                        ).then_inc(f_fu, 1)
            bar_end(vector, "vector")

            bar_wait(vector, 6)
            vector.wait_ge(wd8_l, 16)
            vector.tensor_copy(
                f2_wd[0][:, 0:12, :].rearrange("p c m -> p (c m)"),
                f2_wd8[:, 0:12, :].rearrange("p c m -> p (c m)"),
            ).then_inc(wd8_c, 1)
            for dq in range(4):
                for tt in range(NT_OWN):
                    i = dq * 16 + tt
                    vector.wait_ge(d_mm, i + 1)
                    sd_al.wait(vector, i)
                    sd_sl.wait(vector, dq)
                    if i >= 1:
                        vector.wait_ge(d_ev, i)
                    if i >= 2:
                        sd_st.wait(vector, i - 2)
                    chain(vector.tensor_mul(f2_ytmp, f2_pd[i % 2], f2_sdB[dq % 2]))
                    vector.tensor_add(f2_ysb[i % 2], f2_ytmp, f2_ayt[i % 2]).then_inc(d_ev, 1)
            bar_end(vector, "vector")

        @block.tensor
        def _(tensor):
            tensor.wait_ge(g_id, 16)
            for tt in range(NT_CTX):
                tensor.wait_ge(p1_xnr, tt + 1)
                if tt >= 1:
                    tensor.wait_ge(p1_ev, 4 * tt)
                par = tt % 2
                for j in range(16):
                    mm = tensor.matmul(
                        p1_pT[j // 4][:, (j % 4) * 128:(j % 4 + 1) * 128],
                        p1_xn[par][:, j * 128:(j + 1) * 128],
                        s_ident[:],
                        start=True, stop=True,
                    )
                mm.then_inc(p1_tt, 1)
            bar_end(tensor, "tensor")

            bar_wait(tensor, 1)
            tensor.wait_ge(q_wl, 16 * 16)
            tensor.wait_ge(q_tb, 32)
            for tc in range(NC_OWN):
                sq_ld.wait(tensor, tc)
                for qt in range(16):
                    i = tc * 16 + qt
                    if i >= 4:
                        tensor.wait_ge(q_rp, i - 3)
                    pq = p2_pq[i % 4]
                    for dc in range(16):
                        mm = tensor.matmul(
                            pq, wq_bf[dc][:, qt * 128:(qt + 1) * 128],
                            p2_xnc[tc % 2][:, dc, :],
                            start=(dc == 0), stop=(dc == 15),
                        )
                    mm.then_inc(q_mm, 1)
            bar_end(tensor, "tensor")

            tensor.wait_ge(q_rp, 64)
            tensor.wait_ge(k_wl, 16 * 16)
            for tc in range(NC_CTX):
                skv_ld.wait(tensor, tc)
                for kt in range(16):
                    i = tc * 16 + kt
                    if i >= 4:
                        tensor.wait_ge(kv_rp, i - 3)
                    pk = p2_pq[i % 4]
                    for dc in range(16):
                        mm = tensor.matmul(
                            pk, wk_bf[dc][:, kt * 128:(kt + 1) * 128],
                            p2_xnc[tc % 2][:, dc, :],
                            start=(dc == 0), stop=(dc == 15),
                        )
                    mm.then_inc(kv_km, 1)
                if tc == 0:
                    tensor.wait_ge(v_wl, 16 * 16)
                for vt in range(4):
                    for vc in range(4):
                        jj = (tc * 4 + vt) * 4 + vc
                        if jj >= 2:
                            tensor.wait_ge(kv_ve, jj - 1)
                        for dc in range(16):
                            mm = tensor.matmul(
                                p2_pv[jj % 2],
                                p2_xnc[tc % 2][:, dc, vt * 128:(vt + 1) * 128],
                                wv_bf[dc][:, vc * 512:(vc + 1) * 512],
                                start=(dc == 0), stop=(dc == 15),
                            )
                        mm.then_inc(kv_vm, 1)
            bar_end(tensor, "tensor")

            tensor.wait_ge(kv_rp, 128)
            tensor.wait_ge(kv_ve, 128)
            def p3_mmsv(p):
                pidx = p // 16
                ppr = p % 16
                ph = pidx // 4
                tensor.wait_ge(a_ex, p + 1)
                ae = p3_ae[ppr % 2]
                tensor.matmul(p3_pS[pidx % 2], s_ones[:], ae[:, 0:512],
                              start=(ppr == 0), stop=False)
                tensor.matmul(p3_pS[pidx % 2], s_ones[:], ae[:, 512:1024],
                              start=False, stop=(ppr == 15))
                tensor.matmul(p3_pO[pidx % 2], p3_v[ph % 2][:, 2 * ppr, :], ae[:, 0:512],
                              start=(ppr == 0), stop=False)
                tensor.matmul(p3_pO[pidx % 2], p3_v[ph % 2][:, 2 * ppr + 1, :], ae[:, 512:1024],
                              start=False, stop=(ppr == 15)).then_inc(a_vm, 1)

            for h in range(H):
                sa_hl.wait(tensor, h)
                for qc in range(4):
                    idx = h * 4 + qc
                    if idx >= 2:
                        tensor.wait_ge(a_od, idx - 1)
                        tensor.wait_ge(a_rc, idx - 1)
                    qsl = slice(qc * 512, (qc + 1) * 512)
                    for pr in range(16):
                        p = idx * 16 + pr
                        if p >= 2:
                            tensor.wait_ge(a_ex, p - 1)
                        pl = p3_pl[pr % 2]
                        tensor.matmul(pl[:, 0:512], p3_kt[h % 2][:, (2 * pr) * 128:(2 * pr + 1) * 128],
                                      p3_qt[h % 2][:, qsl], start=True, stop=True)
                        tensor.matmul(pl[:, 512:1024], p3_kt[h % 2][:, (2 * pr + 1) * 128:(2 * pr + 2) * 128],
                                      p3_qt[h % 2][:, qsl], start=True, stop=True).then_inc(a_lg, 1)
                        if p >= 1:
                            p3_mmsv(p - 1)
            p3_mmsv(64 * 16 - 1)
            bar_end(tensor, "tensor")

            tensor.wait_ge(a_ex, 1024)
            tensor.wait_ge(a_od, 64)
            tensor.wait_ge(o_wl, 16 * 16)
            for tt in range(NT_OWN):
                for dc in range(4):
                    i = tt * 4 + dc
                    if i >= 2:
                        tensor.wait_ge(o_ev, i - 1)
                    for hh in range(16):
                        mm = tensor.matmul(
                            p4_po[i % 2], p3_ot[hh][:, tt * 128:(tt + 1) * 128],
                            wout_bf[hh][:, dc * 512:(dc + 1) * 512],
                            start=(hh == 0), stop=(hh == 15),
                        )
                    mm.then_inc(o_mm, 1)
            bar_end(tensor, "tensor")

            bar_wait(tensor, 5)
            tensor.wait_ge(f_pl, 16 * 18)
            for itb in range(16):
                sf_wl.wait(tensor, itb)
                for it in range(4):
                    for tc in range(4):
                        i = (itb * 4 + it) * 4 + tc
                        csl = slice(tc * 512, (tc + 1) * 512)
                        if i >= 4:
                            tensor.wait_ge(f_si, i - 3)
                        for dc in range(16):
                            mm = tensor.matmul(
                                f1_pg[i % 4], f1_wg[itb % 2][:, dc, it * 128:(it + 1) * 128],
                                f1_xno[dc][:, csl],
                                start=(dc == 0), stop=(dc == 15),
                            )
                        mm.then_inc(f_gm, 1)
                        if i >= 4:
                            tensor.wait_ge(f_fu, i - 3)
                        for dc in range(16):
                            mm = tensor.matmul(
                                f1_pu[i % 4], f1_wu[itb % 2][:, dc, it * 128:(it + 1) * 128],
                                f1_xno[dc][:, csl],
                                start=(dc == 0), stop=(dc == 15),
                            )
                        mm.then_inc(f_mm, 1)
            bar_end(tensor, "tensor")

            tensor.wait_ge(f_si, 256)
            for dq in range(4):
                if dq == 0:
                    tensor.wait_ge(wd8_c, 3)
                    sd_wl.wait(tensor, 0)
                else:
                    sd_wl.wait(tensor, dq)
                for tt in range(NT_OWN):
                    i = dq * 16 + tt
                    sd_fl.wait(tensor, i)
                    if i >= 2:
                        tensor.wait_ge(d_ev, i - 1)
                    for ic in range(64):
                        mm = tensor.matmul(
                            f2_pd[i % 2], f2_fc[i % 2][:, ic, :], f2_wd[dq % 2][:, ic, :],
                            start=(ic == 0), stop=(ic == 63),
                        )
                    mm.then_inc(d_mm, 1)
            bar_end(tensor, "tensor")

    return nc


_NC_CACHE = {}


def _get_nc():
    if "nc" not in _NC_CACHE:
        _NC_CACHE["nc"] = build_nc()
    return _NC_CACHE["nc"]


def _rope_tables():
    inv_freq = (1.0 / (10000.0 ** (np.arange(0, DH, 2, dtype=np.float32) / DH))).astype(np.float32)
    t = np.arange(CTX, dtype=np.float32)
    freqs = np.outer(t, inv_freq).astype(np.float32)
    emb = np.concatenate([freqs, freqs], axis=1)
    cos_t = np.cos(emb).T.astype(np.float32).copy()
    sin_t = np.sin(emb).T.astype(np.float32).copy()
    sin_s = sin_t.copy()
    sin_s[0:64] *= -1.0
    return cos_t, sin_s


def kernel(L0_attn_x, L0_attn_w_qkv, L0_attn_w_out, L0_ffn_w_gate, L0_ffn_w_up,
           L0_ffn_w_down, L0_ffn_s_gate, L0_ffn_s_up, L0_ffn_s_down):
    x = np.ascontiguousarray(np.asarray(L0_attn_x, dtype=np.float32))
    wqkv = np.asarray(L0_attn_w_qkv, dtype=np.float32)
    wq_t = np.ascontiguousarray(wqkv[0:DM, :].T)
    wkv_t = np.ascontiguousarray(wqkv[DM:3 * DM, :].T)
    wout_t = np.ascontiguousarray(np.asarray(L0_attn_w_out, dtype=np.float32).T)
    w_gate = np.ascontiguousarray(np.asarray(L0_ffn_w_gate, dtype=np.int8))
    w_up = np.ascontiguousarray(np.asarray(L0_ffn_w_up, dtype=np.int8))
    w_down = np.ascontiguousarray(np.asarray(L0_ffn_w_down, dtype=np.int8))
    sg = np.asarray(L0_ffn_s_gate, dtype=np.float32)
    su = np.asarray(L0_ffn_s_up, dtype=np.float32)
    sd = np.ascontiguousarray(np.asarray(L0_ffn_s_down, dtype=np.float32))
    sg_pt = np.ascontiguousarray(sg.reshape(INTER // 128, 128).T)
    su_pt = np.ascontiguousarray(su.reshape(INTER // 128, 128).T)
    cos_t, sin_s = _rope_tables()
    ident = np.eye(128, dtype=ml_dtypes.bfloat16)

    in_maps = []
    for core in range(8):
        b, g = core // 2, core % 2
        own = x[b, g * TOK:(g + 1) * TOK]
        oth = x[b, (1 - g) * TOK:(2 - g) * TOK]
        x_ctx = np.ascontiguousarray(np.concatenate([own, oth], axis=0))
        cq = np.ascontiguousarray(cos_t[:, g * TOK:(g + 1) * TOK])
        sq = np.ascontiguousarray(sin_s[:, g * TOK:(g + 1) * TOK])
        ck = np.ascontiguousarray(np.concatenate(
            [cos_t[:, g * TOK:(g + 1) * TOK], cos_t[:, (1 - g) * TOK:(2 - g) * TOK]], axis=1))
        sk = np.ascontiguousarray(np.concatenate(
            [sin_s[:, g * TOK:(g + 1) * TOK], sin_s[:, (1 - g) * TOK:(2 - g) * TOK]], axis=1))
        in_maps.append(dict(
            x_ctx=x_ctx, wq_t=wq_t, wkv_t=wkv_t, wout_t=wout_t,
            w_gate=w_gate, w_up=w_up, w_down=w_down,
            sg_pt=sg_pt, su_pt=su_pt, s_down=sd,
            cosq=cq, sinq=sq, cosk=ck, sink=sk, ident=ident,
        ))

    nc = _get_nc()
    res = run_bass_kernel_spmd(nc, in_maps, core_ids=list(range(8)))
    out = np.empty((B, CTX, DM), dtype=np.float32)
    for core in range(8):
        b, g = core // 2, core % 2
        out[b, g * TOK:(g + 1) * TOK] = np.asarray(res.results[core]["y_out"], dtype=np.float32)
    return out


# version 20
# speedup vs baseline: 1.0445x; 1.0445x over previous
import numpy as np
import ml_dtypes
import concourse.bass as bass
import concourse.mybir as mybir
from concourse.bass_utils import run_bass_kernel_spmd
from contextlib import ExitStack

F32 = mybir.dt.float32
BF16 = mybir.dt.bfloat16
I8 = mybir.dt.int8
U8 = mybir.dt.uint8
AF = mybir.ActivationFunctionType
ALU = mybir.AluOpType
AX = mybir.AxisListType

B = 4
CTX = 4096
TOK = 2048
DM = 2048
H = 16
DH = 128
INTER = 8192
SC = float(1.0 / np.sqrt(DH))
EXP_BIAS = -4.0
EPS = 1e-5

NT_CTX = CTX // 128
NC_CTX = CTX // 512
NC_OWN = TOK // 512
NT_OWN = TOK // 128

ENGS = ["sync", "gpsimd", "scalar", "vector", "tensor"]


class DS:

    def __init__(self, nc, e, name, n=2, per=1):
        self.n = n
        self.per = per
        self.sems = [e(nc.semaphore(f"{name}{i}")) for i in range(n)]

    def inc(self, dma, step):
        dma.then_inc(self.sems[step % self.n], 16)

    def wait(self, eng, step):
        eng.wait_ge(self.sems[step % self.n], 16 * self.per * (step // self.n + 1))

    def drain(self, eng, nsteps):
        for p in range(self.n):
            cnt = (nsteps - 1 - p) // self.n + 1 if nsteps > p else 0
            if cnt > 0:
                eng.wait_ge(self.sems[p], 16 * self.per * cnt)


def build_nc(debug_scratch=False):
    nc = bass.Bass("TRN2", target_bir_lowering=False)
    SCR = "ExternalOutput" if debug_scratch else "Internal"

    x_ctx = nc.dram_tensor("x_ctx", [CTX, DM], F32, kind="ExternalInput").ap()
    wq_t = nc.dram_tensor("wq_t", [DM, DM], F32, kind="ExternalInput").ap()
    wkv_t = nc.dram_tensor("wkv_t", [DM, 2 * DM], F32, kind="ExternalInput").ap()
    wout_t = nc.dram_tensor("wout_t", [DM, DM], F32, kind="ExternalInput").ap()
    w_gate = nc.dram_tensor("w_gate", [DM, INTER], I8, kind="ExternalInput").ap()
    w_up = nc.dram_tensor("w_up", [DM, INTER], I8, kind="ExternalInput").ap()
    w_down = nc.dram_tensor("w_down", [INTER, DM], I8, kind="ExternalInput").ap()
    sg_pt = nc.dram_tensor("sg_pt", [128, INTER // 128], F32, kind="ExternalInput").ap()
    su_pt = nc.dram_tensor("su_pt", [128, INTER // 128], F32, kind="ExternalInput").ap()
    s_down = nc.dram_tensor("s_down", [DM], F32, kind="ExternalInput").ap()
    cosq = nc.dram_tensor("cosq", [128, TOK], F32, kind="ExternalInput").ap()
    sinq = nc.dram_tensor("sinq", [128, TOK], F32, kind="ExternalInput").ap()
    cosk = nc.dram_tensor("cosk", [128, CTX], F32, kind="ExternalInput").ap()
    sink = nc.dram_tensor("sink", [128, CTX], F32, kind="ExternalInput").ap()
    ident = nc.dram_tensor("ident", [128, 128], BF16, kind="ExternalInput").ap()

    y_out = nc.dram_tensor("y_out", [TOK, DM], F32, kind="ExternalOutput").ap()

    xnt_d = nc.dram_tensor("xnt_d", [16, 128, CTX], BF16, kind=SCR).ap()
    qt_d = nc.dram_tensor("qt_d", [H, 128, TOK], BF16, kind=SCR).ap()
    kt_d = nc.dram_tensor("kt_d", [H, 128, CTX], BF16, kind=SCR).ap()
    v_d = nc.dram_tensor("v_d", [CTX, DM], BF16, kind=SCR).ap()
    ayx_d = nc.dram_tensor("ayx_d", [TOK, DM], F32, kind=SCR).ap()
    fuse_d = nc.dram_tensor("fuse_d", [64, 128, TOK], BF16, kind=SCR).ap()
    rcp_d = nc.dram_tensor("rcp_d", [4, 512], F32, kind="Internal").ap()

    with ExitStack() as stack:
        e = stack.enter_context

        aw_g = e(nc.sbuf_tensor("arena_w", [128, 131072], U8))
        AW = aw_g.ap() if hasattr(aw_g, "ap") else aw_g[:]
        as_g = e(nc.sbuf_tensor("arena_s", [128, 65536], U8))
        AS = as_g.ap() if hasattr(as_g, "ap") else as_g[:]

        def cw(off, nbytes, dt):
            return AW[:, off:off + nbytes].bitcast(dt)

        def cs(off, nbytes, dt):
            return AS[:, off:off + nbytes].bitcast(dt)

        s_ident = e(nc.sbuf_tensor("s_ident", [128, 128], BF16))
        s_ones = e(nc.sbuf_tensor("s_ones", [128, 1], BF16))
        s_ebias = e(nc.sbuf_tensor("s_ebias", [128, 1], F32))
        s_eps = e(nc.sbuf_tensor("s_eps", [128, 1], F32))
        s_sum = e(nc.sbuf_tensor("s_sum", [128, 32], F32))
        s_ssq = e(nc.sbuf_tensor("s_ssq", [128, 32], F32))
        s_rstd = e(nc.sbuf_tensor("s_rstd", [128, 32], F32))
        s_nbias = e(nc.sbuf_tensor("s_nbias", [128, 32], F32))
        s_tmp32 = e(nc.sbuf_tensor("s_tmp32", [128, 32], F32))
        s_std32 = e(nc.sbuf_tensor("s_std32", [128, 32], F32))
        s_sgsu = e(nc.sbuf_tensor("s_sgsu", [128, 128], F32))
        s_rcp = e(nc.sbuf_tensor("s_rcp", [1, 1024], F32))
        p1_dummy_g = e(nc.sbuf_tensor("p1_dummy", [128, 2, 2048], BF16))
        _pd = p1_dummy_g.ap() if hasattr(p1_dummy_g, "ap") else p1_dummy_g[:]
        p1_dummy = [_pd[:, 0, :], _pd[:, 1, :]]

        p1_x = [cs(0, 8192, F32), cs(8192, 8192, F32)]
        p1_xA = [cs(0, 32768, F32).rearrange("p (c m) -> p c m", c=4),
                 cs(32768, 32768, F32).rearrange("p (c m) -> p c m", c=4)]
        p1_xn = [cs(16384, 4096, BF16), cs(20480, 4096, BF16)]
        p1_xnsb = [cs(24576, 4096, BF16), cs(28672, 4096, BF16)]


        wq_bf = [cw(dc * 4096, 4096, BF16) for dc in range(16)]
        p2_xnc = [cs(0, 16384, BF16).rearrange("p (c m) -> p c m", c=16),
                  cs(16384, 16384, BF16).rearrange("p (c m) -> p c m", c=16)]
        p2_cosq = cs(32768, 8192, F32)
        p2_sinq = cs(40960, 8192, F32)
        p2_tmp = cs(49152, 2048, F32)
        p2_tmp2 = cs(51200, 2048, F32)
        p2_qbf = [cs(53248, 1024, BF16), cs(54272, 1024, BF16)]

        wk_bf = [cw(65536 + dc * 4096, 4096, BF16) for dc in range(16)]
        wv_bf = [cw(dc * 4096, 4096, BF16) for dc in range(16)]
        p2b_cos = [cs(57344, 2048, F32), cs(59392, 2048, F32)]
        p2b_sin = [cs(61440, 2048, F32), cs(63488, 2048, F32)]
        p2b_kbf = [cs(40960, 1024, BF16), cs(41984, 1024, BF16)]
        p2b_vsb = [cs(43008, 4096, BF16), cs(47104, 4096, BF16)]
        p2b_tmp = cs(51200, 2048, F32)
        p2b_tmp2 = cs(53248, 2048, F32)

        p3_ot = [cw(h * 4096, 4096, BF16) for h in range(H)]
        p3_kt = [cs(0, 8192, BF16), cs(8192, 8192, BF16)]
        p3_v = [cs(16384, 8192, BF16).rearrange("p (c m) -> p c m", c=32),
                cs(24576, 8192, BF16).rearrange("p (c m) -> p c m", c=32)]
        p3_qt = [cs(32768, 4096, BF16), cs(36864, 4096, BF16)]
        p3_ae = [cs(40960, 2048, BF16), cs(43008, 2048, BF16)]
        p3_rcpB = [cs(45056, 2048, F32), cs(47104, 2048, F32)]

        wout_bf = [cw(65536 + oc * 4096, 4096, BF16) for oc in range(16)]
        p4_xot = [cs(0, 8192, F32), cs(8192, 8192, F32)]
        p4_ay = [cs(16384, 8192, F32), cs(24576, 8192, F32)]

        f1_xno = [cw(dc * 4096, 4096, BF16) for dc in range(16)]
        f1_wg = [cw(65536, 16384, BF16).rearrange("p (c m) -> p c m", c=16),
                 cw(65536 + 16384, 16384, BF16).rearrange("p (c m) -> p c m", c=16)]
        f1_wu = [cw(98304, 16384, BF16).rearrange("p (c m) -> p c m", c=16),
                 cw(98304 + 16384, 16384, BF16).rearrange("p (c m) -> p c m", c=16)]
        f1_silu = [cs(0, 2048, F32), cs(2048, 2048, F32), cs(4096, 2048, F32), cs(6144, 2048, F32)]
        f1_fasm = [cs(8192, 4096, BF16), cs(12288, 4096, BF16)]

        f2_wd = [cw(0, 65536, BF16).rearrange("p (c m) -> p c m", c=64),
                 cw(65536, 65536, BF16).rearrange("p (c m) -> p c m", c=64)]
        f2_wd8 = cs(47104, 16384, I8).rearrange("p (c m) -> p c m", c=32)
        f2_fc = [cs(0, 16384, BF16).rearrange("p (c m) -> p c m", c=64),
                 cs(16384, 16384, BF16).rearrange("p (c m) -> p c m", c=64)]
        f2_ayt = [cs(32768, 2048, F32), cs(34816, 2048, F32)]
        f2_ytmp = cs(36864, 2048, F32)
        f2_ysb = [cs(38912, 2048, F32), cs(40960, 2048, F32)]
        f2_sdB = [cs(43008, 2048, F32), cs(45056, 2048, F32)]

        ps_a = e(nc.psum_tensor("ps_a", [128, 1024], F32))
        ps_b = e(nc.psum_tensor("ps_b", [128, 1024], F32))
        ps_c = e(nc.psum_tensor("ps_c", [128, 1024], F32))
        ps_d = e(nc.psum_tensor("ps_d", [128, 512], F32))
        ps_e = e(nc.psum_tensor("ps_e", [128, 512], F32))

        p1_pT = [ps_a[:, 0:512], ps_a[:, 512:1024], ps_b[:, 0:512], ps_b[:, 512:1024]]
        p2_pq = [ps_a[:, 0:512], ps_a[:, 512:1024], ps_b[:, 0:512], ps_b[:, 512:1024]]
        p2_pv = [ps_c[:, 0:512], ps_c[:, 512:1024]]
        p3_pl = [ps_a, ps_b]
        p3_pS = [ps_d[0:1, 0:512], ps_e[0:1, 0:512]]
        p3_pO = [ps_c[:, 0:512], ps_c[:, 512:1024]]
        p4_po = [ps_a[:, 0:512], ps_a[:, 512:1024]]
        f1_pg = [ps_a[:, 0:512], ps_a[:, 512:1024], ps_c[:, 0:512], ps_c[:, 512:1024]]
        f1_pu = [ps_b[:, 0:512], ps_b[:, 512:1024], ps_d[:, :], ps_e[:, :]]
        f2_pd = [ps_a[:, 0:512], ps_a[:, 512:1024]]

        barv = {en: e(nc.semaphore(f"bar_{en}")) for en in ENGS}

        def bar_end(eng, en):
            eng.sem_inc(barv[en], 1)

        def bar_wait(eng, k):
            for en in ENGS:
                eng.wait_ge(barv[en], k)

        g_id = e(nc.semaphore("g_id"))
        q_wl = e(nc.semaphore("q_wl"))
        k_wl = e(nc.semaphore("k_wl"))
        v_wl = e(nc.semaphore("v_wl"))
        q_tb = e(nc.semaphore("q_tb"))
        wd8_l = e(nc.semaphore("wd8_l"))
        wd8_c = e(nc.semaphore("wd8_c"))
        kv_wl = e(nc.semaphore("kv_wl"))
        o_wl = e(nc.semaphore("o_wl"))
        f_pl = e(nc.semaphore("f_pl"))

        sp1_ld = DS(nc, e, "p1ld")
        sp1_la = DS(nc, e, "p1la")
        sp1_sp = DS(nc, e, "p1sp")
        sq_ld = DS(nc, e, "qld")
        sq_sp = DS(nc, e, "qsp")
        skv_ld = DS(nc, e, "kvld", per=3)
        skv_sp = DS(nc, e, "kvsp")
        skv_vsp = DS(nc, e, "kvvsp")
        sa_hl = DS(nc, e, "ahl", per=3)
        sa_bs = DS(nc, e, "abs")
        sa_bb = DS(nc, e, "abb")
        so_xl = DS(nc, e, "oxl")
        so_sp = DS(nc, e, "osp")
        sf_wl = DS(nc, e, "fwl", per=2)
        sf_sp = DS(nc, e, "fsp")
        sd_wl = DS(nc, e, "dwl")
        sd_sl = DS(nc, e, "dsl")
        sd_fl = DS(nc, e, "dfl")
        sd_al = DS(nc, e, "dal")
        sd_st = DS(nc, e, "dst")

        p1_sq = e(nc.semaphore("p1_sq"))
        p1_rd = e(nc.semaphore("p1_rd"))
        p1_st = e(nc.semaphore("p1_st"))
        sch = e(nc.semaphore("sch"))
        p1_v1 = e(nc.semaphore("p1_v1"))
        p1_sd = e(nc.semaphore("p1_sd"))
        p1_v2 = e(nc.semaphore("p1_v2"))
        p1_xnr = e(nc.semaphore("p1_xnr"))
        p1_tt = e(nc.semaphore("p1_tt"))
        p1_ev = e(nc.semaphore("p1_ev"))
        q_mm = e(nc.semaphore("q_mm"))
        q_rp = e(nc.semaphore("q_rp"))
        kv_km = e(nc.semaphore("kv_km"))
        kv_rp = e(nc.semaphore("kv_rp"))
        kv_vm = e(nc.semaphore("kv_vm"))
        kv_ve = e(nc.semaphore("kv_ve"))
        a_lg = e(nc.semaphore("a_lg"))
        a_ex = e(nc.semaphore("a_ex"))
        a_vm = e(nc.semaphore("a_vm"))
        a_rc = e(nc.semaphore("a_rc"))
        a_od = e(nc.semaphore("a_od"))
        o_mm = e(nc.semaphore("o_mm"))
        o_ev = e(nc.semaphore("o_ev"))
        f_gm = e(nc.semaphore("f_gm"))
        f_mm = e(nc.semaphore("f_mm"))
        f_si = e(nc.semaphore("f_si"))
        f_fu = e(nc.semaphore("f_fu"))
        d_mm = e(nc.semaphore("d_mm"))
        d_ev = e(nc.semaphore("d_ev"))
        vch = e(nc.semaphore("vch"))

        block = e(nc.Block())

        @block.sync
        def _(sync):
            for s in range(0, 8, 2):
                if s >= 2:
                    sync.wait_ge(p1_sq, 4 * (s - 1))
                    sync.wait_ge(p1_rd, 4 * (s - 1))
                sp1_la.inc(sync.dma_start(
                    p1_xA[s % 2],
                    x_ctx[s * 512:(s + 1) * 512, :].rearrange("(c p) m -> p c m", p=128),
                ), s)
            for tt in range(NT_CTX):
                if tt < 2:
                    sync.wait_ge(p1_sq, NT_CTX)
                    sync.wait_ge(p1_rd, NT_CTX)
                else:
                    sync.wait_ge(p1_xnr, tt - 1)
                sp1_ld.inc(sync.dma_start(p1_x[tt % 2], x_ctx[tt * 128:(tt + 1) * 128, :]), tt)
            sync.dma_start(p2_cosq, cosq).then_inc(q_tb, 16)
            sync.dma_start(p2_sinq, sinq).then_inc(q_tb, 16)
            bar_end(sync, "sync")
            bar_wait(sync, 1)
            for tc in range(NC_OWN):
                if tc >= 2:
                    sync.wait_ge(q_rp, (tc - 1) * 16)
                sq_ld.inc(sync.dma_start(
                    p2_xnc[tc % 2],
                    xnt_d[:, :, tc * 512:(tc + 1) * 512].transpose([1, 0, 2]),
                ), tc)
            bar_end(sync, "sync")
            for tc in range(NC_CTX):
                if tc == 0:
                    sync.wait_ge(q_mm, 48)
                elif tc == 1:
                    sync.wait_ge(q_mm, 64)
                else:
                    sync.wait_ge(kv_rp, (tc - 1) * 16)
                    sync.wait_ge(kv_vm, (tc - 1) * 16)
                skv_ld.inc(sync.dma_start(
                    p2_xnc[tc % 2],
                    xnt_d[:, :, tc * 512:(tc + 1) * 512].transpose([1, 0, 2]),
                ), tc)
                skv_ld.inc(sync.dma_start(p2b_cos[tc % 2], cosk[:, tc * 512:(tc + 1) * 512]), tc)
                skv_ld.inc(sync.dma_start(p2b_sin[tc % 2], sink[:, tc * 512:(tc + 1) * 512]), tc)
            bar_end(sync, "sync")
            sync.wait_ge(kv_km, 128)
            sync.wait_ge(kv_vm, 128)
            skv_sp.drain(sync, 128)
            skv_vsp.drain(sync, 32)
            sq_sp.drain(sync, 64)

            def p3_head_load(h):
                if h >= 2:
                    sync.wait_ge(a_od, 4 * (h - 1))
                sa_hl.inc(sync.dma_start(p3_kt[h % 2], kt_d[h]), h)
                sa_hl.inc(sync.dma_start(
                    p3_v[h % 2],
                    v_d[:, h * 128:(h + 1) * 128].rearrange("(c p) d -> p c d", p=128),
                ), h)
                sa_hl.inc(sync.dma_start(p3_qt[h % 2], qt_d[h]), h)

            p3_head_load(0)
            for h in range(H):
                if h + 1 < H:
                    p3_head_load(h + 1)
                for qc in range(4):
                    idx = h * 4 + qc
                    sync.wait_ge(a_rc, idx + 1)
                    sa_bs.inc(sync.dma_start(
                        rcp_d[idx % 4, :][None, :],
                        s_rcp[0:1, (idx % 2) * 512:(idx % 2) * 512 + 512],
                    ), idx)
                    sa_bs.wait(sync, idx)
                    sa_bb.inc(sync.dma_start(
                        p3_rcpB[idx % 2], rcp_d[idx % 4, :][None, :].partition_broadcast(128)
                    ), idx)
            bar_end(sync, "sync")
            sync.wait_ge(a_vm, 1024)
            sync.wait_ge(a_od, 64)
            for tt in range(NT_OWN):
                if tt >= 2:
                    sync.wait_ge(o_ev, 4 * (tt - 1))
                so_xl.inc(sync.dma_start(p4_xot[tt % 2], x_ctx[tt * 128:(tt + 1) * 128, :]), tt)
            sync.wait_ge(o_mm, 64)
            for dc in range(16):
                sync.dma_start(f1_xno[dc], xnt_d[dc, :, 0:TOK]).then_inc(f_pl, 16)
            sync.dma_start(s_sgsu[:, 0:64], sg_pt).then_inc(f_pl, 16)
            sync.dma_start(s_sgsu[:, 64:128], su_pt).then_inc(f_pl, 16)
            bar_end(sync, "sync")
            bar_wait(sync, 5)
            sync.dma_start(
                f2_wd8, w_down[0:4096, 0:512].rearrange("(c p) m -> p c m", p=128)
            ).then_inc(wd8_l, 16)
            bar_end(sync, "sync")
            sync.wait_ge(f_fu, 256)
            for p in range(2):
                sync.wait_ge(sf_sp.sems[p], 16 * 32)
                sync.wait_ge(so_sp.sems[p], 16 * 8)
            for dq in range(4):
                if dq >= 2:
                    sync.wait_ge(d_ev, 16 * (dq - 1))
                sd_sl.inc(sync.dma_start(
                    f2_sdB[dq % 2],
                    s_down[dq * 512:(dq + 1) * 512][None, :].partition_broadcast(128),
                ), dq)
                for tt in range(NT_OWN):
                    i = dq * 16 + tt
                    if i >= 2:
                        sync.wait_ge(d_mm, i - 1)
                        sync.wait_ge(d_ev, i - 1)
                    sd_fl.inc(sync.dma_start(
                        f2_fc[i % 2], fuse_d[:, :, tt * 128:(tt + 1) * 128].transpose([1, 0, 2])
                    ), i)
                    sd_al.inc(sync.dma_start(
                        f2_ayt[i % 2], ayx_d[tt * 128:(tt + 1) * 128, dq * 512:(dq + 1) * 512]
                    ), i)
                    if i >= 1:
                        j = i - 1
                        sync.wait_ge(d_ev, j + 1)
                        sd_st.inc(sync.dma_start(
                            y_out[(j % 16) * 128:(j % 16 + 1) * 128,
                                  (j // 16) * 512:(j // 16 + 1) * 512], f2_ysb[j % 2]
                        ), j)
            sync.wait_ge(d_ev, 64)
            sd_st.inc(sync.dma_start(
                y_out[15 * 128:16 * 128, 3 * 512:4 * 512], f2_ysb[63 % 2]
            ), 63)
            sd_st.drain(sync, 64)
            bar_end(sync, "sync")

        @block.gpsimd
        def _(gpsimd):
            gpsimd.dma_start(s_ident[:], ident).then_inc(g_id, 16)
            for dc in range(16):
                gpsimd.dma_start(wq_bf[dc], wq_t[dc * 128:(dc + 1) * 128, :]).then_inc(q_wl, 16)
            for dc in range(16):
                gpsimd.dma_start(wk_bf[dc], wkv_t[dc * 128:(dc + 1) * 128, 0:2048]).then_inc(k_wl, 16)
            bar_end(gpsimd, "gpsimd")
            bar_wait(gpsimd, 1)
            gpsimd.wait_ge(q_mm, 64)
            for dc in range(16):
                gpsimd.dma_start(wv_bf[dc], wkv_t[dc * 128:(dc + 1) * 128, 2048:4096]).then_inc(v_wl, 16)
            bar_end(gpsimd, "gpsimd")
            bar_wait(gpsimd, 2)
            gpsimd.wait_ge(kv_km, 128)
            for oc in range(16):
                gpsimd.dma_start(wout_bf[oc], wout_t[oc * 128:(oc + 1) * 128, :]).then_inc(o_wl, 16)
            bar_end(gpsimd, "gpsimd")
            bar_wait(gpsimd, 3)
            bar_end(gpsimd, "gpsimd")
            bar_wait(gpsimd, 4)
            gpsimd.wait_ge(o_mm, 64)
            for itb in range(2):
                sf_wl.inc(gpsimd.dma_start(
                    f1_wg[itb % 2], w_gate[:, itb * 512:(itb + 1) * 512].rearrange("(c p) m -> p c m", p=128)
                ), itb)
                sf_wl.inc(gpsimd.dma_start(
                    f1_wu[itb % 2], w_up[:, itb * 512:(itb + 1) * 512].rearrange("(c p) m -> p c m", p=128)
                ), itb)
            bar_end(gpsimd, "gpsimd")
            bar_wait(gpsimd, 5)
            for itb in range(2, 16):
                gpsimd.wait_ge(f_mm, 16 * (itb - 1))
                sf_wl.inc(gpsimd.dma_start(
                    f1_wg[itb % 2], w_gate[:, itb * 512:(itb + 1) * 512].rearrange("(c p) m -> p c m", p=128)
                ), itb)
                sf_wl.inc(gpsimd.dma_start(
                    f1_wu[itb % 2], w_up[:, itb * 512:(itb + 1) * 512].rearrange("(c p) m -> p c m", p=128)
                ), itb)
            gpsimd.wait_ge(f_mm, 256)
            gpsimd.wait_ge(wd8_l, 16)
            sd_wl.inc(gpsimd.dma_start(
                f2_wd[0][:, 32:64, :],
                w_down[4096:8192, 0:512].rearrange("(c p) m -> p c m", p=128)
            ), 0)
            gpsimd.tensor_copy(
                f2_wd[0][:, 12:22, :].rearrange("p c m -> p (c m)"),
                f2_wd8[:, 12:22, :].rearrange("p c m -> p (c m)"),
            ).then_inc(wd8_c, 1)
            sd_wl.inc(gpsimd.dma_start(
                f2_wd[1], w_down[:, 512:1024].rearrange("(c p) m -> p c m", p=128)
            ), 1)
            bar_end(gpsimd, "gpsimd")
            bar_wait(gpsimd, 6)
            for dq in range(2, 4):
                gpsimd.wait_ge(d_mm, 16 * (dq - 1))
                sd_wl.inc(gpsimd.dma_start(
                    f2_wd[dq % 2], w_down[:, dq * 512:(dq + 1) * 512].rearrange("(c p) m -> p c m", p=128)
                ), dq)
            bar_end(gpsimd, "gpsimd")

        @block.scalar
        def _(scalar):
            for s in range(8):
                if s % 2 == 1:
                    if s >= 2:
                        scalar.wait_ge(p1_sq, 4 * (s - 1))
                        scalar.wait_ge(p1_rd, 4 * (s - 1))
                    sp1_la.inc(scalar.dma_start(
                        p1_xA[s % 2],
                        x_ctx[s * 512:(s + 1) * 512, :].rearrange("(c p) m -> p c m", p=128),
                    ), s)
                sp1_la.wait(scalar, s)
                for c in range(4):
                    tt = 4 * s + c
                    if tt >= 2:
                        scalar.wait_ge(p1_sq, tt - 1)
                    scalar.activation(
                        p1_dummy[tt % 2], p1_xA[s % 2][:, c, :], AF.Square,
                        accum_out=s_ssq[:, tt:tt + 1],
                    ).then_inc(p1_sq, 1)
            scalar.wait_ge(p1_st, 1)
            scalar.activation(s_std32[:], s_tmp32[:], AF.Sqrt, bias=s_eps[:]).then_inc(p1_sd, 1)
            def p1_norm_emit(tt):
                sp1_ld.wait(scalar, tt)
                if tt >= 2:
                    scalar.wait_ge(p1_tt, tt - 1)
                scalar.activation(
                    p1_xn[tt % 2], p1_x[tt % 2], AF.Identity,
                    scale=s_rstd[:, tt:tt + 1], bias=s_nbias[:, tt:tt + 1],
                ).then_inc(p1_xnr, 1)

            def p1_spill_emit(tt):
                scalar.wait_ge(p1_ev, 4 * (tt + 1))
                sp1_sp.inc(scalar.dma_start(
                    xnt_d[:, :, tt * 128:(tt + 1) * 128].transpose([1, 0, 2]),
                    p1_xnsb[tt % 2].rearrange("p (c m) -> p c m", c=16),
                ), tt)

            scalar.wait_ge(p1_st, 2)
            p1_norm_emit(0)
            p1_norm_emit(1)
            for tt in range(2, NT_CTX):
                p1_norm_emit(tt)
                p1_spill_emit(tt - 2)
            p1_spill_emit(NT_CTX - 2)
            p1_spill_emit(NT_CTX - 1)
            sp1_sp.drain(scalar, NT_CTX)
            bar_end(scalar, "scalar")

            bar_wait(scalar, 1)
            for tc in range(NC_OWN):
                for qt in range(16):
                    i = tc * 16 + qt
                    scalar.wait_ge(q_rp, i + 1)
                    sq_sp.inc(scalar.dma_start(
                        qt_d[qt][:, tc * 512:(tc + 1) * 512], p2_qbf[i % 2]
                    ), i)
            sq_sp.drain(scalar, 64)
            bar_end(scalar, "scalar")

            for tc in range(NC_CTX):
                for kt in range(16):
                    i = tc * 16 + kt
                    scalar.wait_ge(kv_rp, i + 1)
                    skv_sp.inc(scalar.dma_start(
                        kt_d[kt][:, tc * 512:(tc + 1) * 512], p2b_kbf[i % 2]
                    ), i)
                for vt in range(4):
                    j = tc * 4 + vt
                    scalar.wait_ge(kv_ve, 4 * (j + 1))
                    skv_vsp.inc(scalar.dma_start(
                        v_d[j * 128:(j + 1) * 128, :], p2b_vsb[j % 2]
                    ), j)
            skv_sp.drain(scalar, 128)
            skv_vsp.drain(scalar, 32)
            bar_end(scalar, "scalar")

            for h in range(H):
                for qc in range(4):
                    idx = h * 4 + qc
                    for pr in range(16):
                        p = idx * 16 + pr
                        scalar.wait_ge(a_lg, p + 1)
                        if p >= 2:
                            scalar.wait_ge(a_vm, p - 1)
                        scalar.activation(
                            p3_ae[pr % 2], p3_pl[pr % 2][:], AF.Exp,
                            bias=s_ebias[:], scale=SC,
                        ).then_inc(a_ex, 1)
            bar_end(scalar, "scalar")

            for tt in range(NT_OWN):
                scalar.wait_ge(o_ev, 4 * (tt + 1))
                so_sp.inc(scalar.dma_start(
                    ayx_d[tt * 128:(tt + 1) * 128, :], p4_ay[tt % 2]
                ), tt)
            so_sp.drain(scalar, NT_OWN)
            bar_end(scalar, "scalar")

            bar_wait(scalar, 5)
            for itb in range(16):
                for it in range(4):
                    git = itb * 4 + it
                    if git >= 1:
                        scalar.wait_ge(f_fu, 4 * git)
                        sf_sp.inc(scalar.dma_start(fuse_d[git - 1], f1_fasm[(git - 1) % 2]), git - 1)
                    for tc in range(4):
                        i = git * 4 + tc
                        scalar.wait_ge(f_gm, i + 1)
                        if i >= 4:
                            scalar.wait_ge(f_fu, i - 3)
                        scalar.activation(
                            f1_silu[i % 4], f1_pg[i % 4], AF.Silu,
                            scale=s_sgsu[:, itb * 4 + it: itb * 4 + it + 1],
                        ).then_inc(f_si, 1)
            scalar.wait_ge(f_fu, 4 * 64)
            sf_sp.inc(scalar.dma_start(fuse_d[63], f1_fasm[63 % 2]), 63)
            sf_sp.drain(scalar, 64)
            bar_end(scalar, "scalar")

            bar_wait(scalar, 6)
            scalar.wait_ge(wd8_l, 16)
            scalar.copy(
                f2_wd[0][:, 22:32, :].rearrange("p c m -> p (c m)"),
                f2_wd8[:, 22:32, :].rearrange("p c m -> p (c m)"),
            ).then_inc(wd8_c, 1)
            bar_end(scalar, "scalar")

        @block.vector
        def _(vector):
            _vch = [0]

            def chain(ins, n=1):
                ins.then_inc(vch, 1)
                _vch[0] += 1
                vector.wait_ge(vch, _vch[0] + (n - 1) * 0)

            vector.memset(s_ones[:], 1.0).then_inc(vch, 1)
            vector.memset(s_ebias[:], float(EXP_BIAS)).then_inc(vch, 1)
            vector.memset(s_eps[:], float(EPS)).then_inc(vch, 1)
            _vch[0] += 3
            vector.wait_ge(vch, _vch[0])

            for s in range(8):
                sp1_la.wait(vector, s)
                for c in range(4):
                    tt = 4 * s + c
                    vector.reduce_sum(s_sum[:, tt:tt + 1], p1_xA[s % 2][:, c, :], axis=AX.X).then_inc(p1_rd, 1)
            vector.wait_ge(p1_sq, NT_CTX)
            vector.wait_ge(p1_rd, NT_CTX)
            chain(vector.tensor_mul(s_tmp32[:], s_sum[:], s_sum[:]))
            chain(vector.tensor_scalar_mul(s_tmp32[:], s_tmp32[:], 1.0 / (DM * DM)))
            vector.scalar_tensor_tensor(
                s_tmp32[:], s_ssq[:], 1.0 / DM, s_tmp32[:], ALU.mult, ALU.subtract
            ).then_inc(p1_st, 1)
            vector.wait_ge(p1_sd, 1)
            chain(vector.reciprocal(s_rstd[:], s_std32[:]))
            chain(vector.tensor_mul(s_nbias[:], s_sum[:], s_rstd[:]))
            vector.tensor_scalar_mul(s_nbias[:], s_nbias[:], -1.0 / DM).then_inc(p1_st, 1)

            def p1_evac_emit(tt):
                par = tt % 2
                vector.wait_ge(p1_tt, tt + 1)
                if tt >= 2:
                    sp1_sp.wait(vector, tt - 2)
                for k in range(4):
                    vector.tensor_copy(
                        p1_xnsb[par][:, k * 512:(k + 1) * 512], p1_pT[k]
                    ).then_inc(p1_ev, 1)

            for tt in range(NT_CTX):
                p1_evac_emit(tt)
            bar_end(vector, "vector")

            bar_wait(vector, 1)
            for tc in range(NC_OWN):
                csl = slice(tc * 512, (tc + 1) * 512)
                for qt in range(16):
                    i = tc * 16 + qt
                    vector.wait_ge(q_mm, i + 1)
                    if i >= 1:
                        vector.wait_ge(q_rp, i)
                    if i >= 2:
                        sq_sp.wait(vector, i - 2)
                    pq = p2_pq[i % 4]
                    vector.tensor_mul(p2_tmp[0:64, :], pq[64:128, :], p2_sinq[0:64, csl]).then_inc(vch, 1)
                    vector.tensor_mul(p2_tmp[64:128, :], pq[0:64, :], p2_sinq[64:128, csl]).then_inc(vch, 1)
                    vector.tensor_mul(p2_tmp2[:], pq[:], p2_cosq[:, csl]).then_inc(vch, 1)
                    _vch[0] += 3
                    vector.wait_ge(vch, _vch[0])
                    vector.tensor_add(p2_qbf[i % 2], p2_tmp[:], p2_tmp2[:]).then_inc(q_rp, 1)
            bar_end(vector, "vector")

            sq_sp.drain(vector, 64)
            for tc in range(NC_CTX):
                for kt in range(16):
                    i = tc * 16 + kt
                    vector.wait_ge(kv_km, i + 1)
                    if i >= 1:
                        vector.wait_ge(kv_rp, i)
                    if i >= 2:
                        skv_sp.wait(vector, i - 2)
                    pk = p2_pq[i % 4]
                    vector.tensor_mul(p2b_tmp[0:64, :], pk[64:128, :], p2b_sin[tc % 2][0:64, :]).then_inc(vch, 1)
                    vector.tensor_mul(p2b_tmp[64:128, :], pk[0:64, :], p2b_sin[tc % 2][64:128, :]).then_inc(vch, 1)
                    vector.tensor_mul(p2b_tmp2[:], pk[:], p2b_cos[tc % 2]).then_inc(vch, 1)
                    _vch[0] += 3
                    vector.wait_ge(vch, _vch[0])
                    vector.tensor_add(p2b_kbf[i % 2], p2b_tmp[:], p2b_tmp2[:]).then_inc(kv_rp, 1)
                for vt in range(4):
                    j = tc * 4 + vt
                    if j >= 2:
                        skv_vsp.wait(vector, j - 2)
                    for vc in range(4):
                        jj = j * 4 + vc
                        vector.wait_ge(kv_vm, jj + 1)
                        vector.tensor_copy(
                            p2b_vsb[j % 2][:, vc * 512:(vc + 1) * 512], p2_pv[jj % 2]
                        ).then_inc(kv_ve, 1)
            bar_end(vector, "vector")

            for h in range(H):
                for qc in range(4):
                    idx = h * 4 + qc
                    vector.wait_ge(a_vm, 16 * (idx + 1))
                    if idx >= 2:
                        sa_bs.wait(vector, idx - 2)
                    vector.reciprocal(
                        s_rcp[0:1, (idx % 2) * 512:(idx % 2) * 512 + 512], p3_pS[idx % 2]
                    ).then_inc(a_rc, 1)
                    sa_bb.wait(vector, idx)
                    vector.tensor_mul(
                        p3_ot[h][:, qc * 512:(qc + 1) * 512], p3_pO[idx % 2], p3_rcpB[idx % 2]
                    ).then_inc(a_od, 1)
            bar_end(vector, "vector")

            vector.wait_ge(a_vm, 1024)
            for tt in range(NT_OWN):
                for dc in range(4):
                    i = tt * 4 + dc
                    vector.wait_ge(o_mm, i + 1)
                    so_xl.wait(vector, tt)
                    if tt >= 2:
                        so_sp.wait(vector, tt - 2)
                    vector.tensor_add(
                        p4_ay[tt % 2][:, dc * 512:(dc + 1) * 512], p4_po[i % 2],
                        p4_xot[tt % 2][:, dc * 512:(dc + 1) * 512],
                    ).then_inc(o_ev, 1)
            bar_end(vector, "vector")

            bar_wait(vector, 5)
            for itb in range(16):
                for it in range(4):
                    git = itb * 4 + it
                    if git >= 2:
                        sf_sp.wait(vector, git - 2)
                    for tc in range(4):
                        i = git * 4 + tc
                        vector.wait_ge(f_si, i + 1)
                        vector.wait_ge(f_mm, i + 1)
                        vector.scalar_tensor_tensor(
                            f1_fasm[git % 2][:, tc * 512:(tc + 1) * 512],
                            f1_pu[i % 4],
                            s_sgsu[:, 64 + itb * 4 + it: 64 + itb * 4 + it + 1],
                            f1_silu[i % 4],
                            ALU.mult, ALU.mult,
                        ).then_inc(f_fu, 1)
            bar_end(vector, "vector")

            bar_wait(vector, 6)
            vector.wait_ge(wd8_l, 16)
            vector.tensor_copy(
                f2_wd[0][:, 0:12, :].rearrange("p c m -> p (c m)"),
                f2_wd8[:, 0:12, :].rearrange("p c m -> p (c m)"),
            ).then_inc(wd8_c, 1)
            for dq in range(4):
                for tt in range(NT_OWN):
                    i = dq * 16 + tt
                    vector.wait_ge(d_mm, i + 1)
                    sd_al.wait(vector, i)
                    sd_sl.wait(vector, dq)
                    if i >= 1:
                        vector.wait_ge(d_ev, i)
                    if i >= 2:
                        sd_st.wait(vector, i - 2)
                    chain(vector.tensor_mul(f2_ytmp, f2_pd[i % 2], f2_sdB[dq % 2]))
                    vector.tensor_add(f2_ysb[i % 2], f2_ytmp, f2_ayt[i % 2]).then_inc(d_ev, 1)
            bar_end(vector, "vector")

        @block.tensor
        def _(tensor):
            tensor.wait_ge(g_id, 16)
            for tt in range(NT_CTX):
                tensor.wait_ge(p1_xnr, tt + 1)
                if tt >= 1:
                    tensor.wait_ge(p1_ev, 4 * tt)
                par = tt % 2
                for j in range(16):
                    mm = tensor.matmul(
                        p1_pT[j // 4][:, (j % 4) * 128:(j % 4 + 1) * 128],
                        p1_xn[par][:, j * 128:(j + 1) * 128],
                        s_ident[:],
                        start=True, stop=True,
                    )
                mm.then_inc(p1_tt, 1)
            bar_end(tensor, "tensor")

            bar_wait(tensor, 1)
            tensor.wait_ge(q_wl, 16 * 16)
            tensor.wait_ge(q_tb, 32)
            for tc in range(NC_OWN):
                sq_ld.wait(tensor, tc)
                for qt in range(16):
                    i = tc * 16 + qt
                    if i >= 4:
                        tensor.wait_ge(q_rp, i - 3)
                    pq = p2_pq[i % 4]
                    for dc in range(16):
                        mm = tensor.matmul(
                            pq, wq_bf[dc][:, qt * 128:(qt + 1) * 128],
                            p2_xnc[tc % 2][:, dc, :],
                            start=(dc == 0), stop=(dc == 15),
                        )
                    mm.then_inc(q_mm, 1)
            bar_end(tensor, "tensor")

            tensor.wait_ge(q_rp, 64)
            tensor.wait_ge(k_wl, 16 * 16)
            for tc in range(NC_CTX):
                skv_ld.wait(tensor, tc)
                for kt in range(16):
                    i = tc * 16 + kt
                    if i >= 4:
                        tensor.wait_ge(kv_rp, i - 3)
                    pk = p2_pq[i % 4]
                    for dc in range(16):
                        mm = tensor.matmul(
                            pk, wk_bf[dc][:, kt * 128:(kt + 1) * 128],
                            p2_xnc[tc % 2][:, dc, :],
                            start=(dc == 0), stop=(dc == 15),
                        )
                    mm.then_inc(kv_km, 1)
                if tc == 0:
                    tensor.wait_ge(v_wl, 16 * 16)
                for vt in range(4):
                    for vc in range(4):
                        jj = (tc * 4 + vt) * 4 + vc
                        if jj >= 2:
                            tensor.wait_ge(kv_ve, jj - 1)
                        for dc in range(16):
                            mm = tensor.matmul(
                                p2_pv[jj % 2],
                                p2_xnc[tc % 2][:, dc, vt * 128:(vt + 1) * 128],
                                wv_bf[dc][:, vc * 512:(vc + 1) * 512],
                                start=(dc == 0), stop=(dc == 15),
                            )
                        mm.then_inc(kv_vm, 1)
            bar_end(tensor, "tensor")

            tensor.wait_ge(kv_rp, 128)
            tensor.wait_ge(kv_ve, 128)
            def p3_mmsv(p):
                pidx = p // 16
                ppr = p % 16
                ph = pidx // 4
                tensor.wait_ge(a_ex, p + 1)
                ae = p3_ae[ppr % 2]
                tensor.matmul(p3_pS[pidx % 2], s_ones[:], ae[:, 0:512],
                              start=(ppr == 0), stop=False)
                tensor.matmul(p3_pS[pidx % 2], s_ones[:], ae[:, 512:1024],
                              start=False, stop=(ppr == 15))
                tensor.matmul(p3_pO[pidx % 2], p3_v[ph % 2][:, 2 * ppr, :], ae[:, 0:512],
                              start=(ppr == 0), stop=False)
                tensor.matmul(p3_pO[pidx % 2], p3_v[ph % 2][:, 2 * ppr + 1, :], ae[:, 512:1024],
                              start=False, stop=(ppr == 15)).then_inc(a_vm, 1)

            for h in range(H):
                sa_hl.wait(tensor, h)
                for qc in range(4):
                    idx = h * 4 + qc
                    if idx >= 2:
                        tensor.wait_ge(a_od, idx - 1)
                        tensor.wait_ge(a_rc, idx - 1)
                    qsl = slice(qc * 512, (qc + 1) * 512)
                    for pr in range(16):
                        p = idx * 16 + pr
                        if p >= 2:
                            tensor.wait_ge(a_ex, p - 1)
                        pl = p3_pl[pr % 2]
                        tensor.matmul(pl[:, 0:512], p3_kt[h % 2][:, (2 * pr) * 128:(2 * pr + 1) * 128],
                                      p3_qt[h % 2][:, qsl], start=True, stop=True)
                        tensor.matmul(pl[:, 512:1024], p3_kt[h % 2][:, (2 * pr + 1) * 128:(2 * pr + 2) * 128],
                                      p3_qt[h % 2][:, qsl], start=True, stop=True).then_inc(a_lg, 1)
                        if p >= 1:
                            p3_mmsv(p - 1)
            p3_mmsv(64 * 16 - 1)
            bar_end(tensor, "tensor")

            tensor.wait_ge(a_ex, 1024)
            tensor.wait_ge(a_od, 64)
            tensor.wait_ge(o_wl, 16 * 16)
            for tt in range(NT_OWN):
                for dc in range(4):
                    i = tt * 4 + dc
                    if i >= 2:
                        tensor.wait_ge(o_ev, i - 1)
                    for hh in range(16):
                        mm = tensor.matmul(
                            p4_po[i % 2], p3_ot[hh][:, tt * 128:(tt + 1) * 128],
                            wout_bf[hh][:, dc * 512:(dc + 1) * 512],
                            start=(hh == 0), stop=(hh == 15),
                        )
                    mm.then_inc(o_mm, 1)
            bar_end(tensor, "tensor")

            bar_wait(tensor, 5)
            tensor.wait_ge(f_pl, 16 * 18)
            for itb in range(16):
                sf_wl.wait(tensor, itb)
                for it in range(4):
                    for tc in range(4):
                        i = (itb * 4 + it) * 4 + tc
                        csl = slice(tc * 512, (tc + 1) * 512)
                        if i >= 4:
                            tensor.wait_ge(f_si, i - 3)
                        for dc in range(16):
                            mm = tensor.matmul(
                                f1_pg[i % 4], f1_wg[itb % 2][:, dc, it * 128:(it + 1) * 128],
                                f1_xno[dc][:, csl],
                                start=(dc == 0), stop=(dc == 15),
                            )
                        mm.then_inc(f_gm, 1)
                        if i >= 4:
                            tensor.wait_ge(f_fu, i - 3)
                        for dc in range(16):
                            mm = tensor.matmul(
                                f1_pu[i % 4], f1_wu[itb % 2][:, dc, it * 128:(it + 1) * 128],
                                f1_xno[dc][:, csl],
                                start=(dc == 0), stop=(dc == 15),
                            )
                        mm.then_inc(f_mm, 1)
            bar_end(tensor, "tensor")

            tensor.wait_ge(f_si, 256)
            for dq in range(4):
                if dq == 0:
                    tensor.wait_ge(wd8_c, 3)
                    sd_wl.wait(tensor, 0)
                else:
                    sd_wl.wait(tensor, dq)
                for tt in range(NT_OWN):
                    i = dq * 16 + tt
                    sd_fl.wait(tensor, i)
                    if i >= 2:
                        tensor.wait_ge(d_ev, i - 1)
                    for ic in range(64):
                        mm = tensor.matmul(
                            f2_pd[i % 2], f2_fc[i % 2][:, ic, :], f2_wd[dq % 2][:, ic, :],
                            start=(ic == 0), stop=(ic == 63),
                        )
                    mm.then_inc(d_mm, 1)
            bar_end(tensor, "tensor")

    return nc


_NC_CACHE = {}


def _get_nc():
    if "nc" not in _NC_CACHE:
        _NC_CACHE["nc"] = build_nc()
    return _NC_CACHE["nc"]


def _rope_tables():
    inv_freq = (1.0 / (10000.0 ** (np.arange(0, DH, 2, dtype=np.float32) / DH))).astype(np.float32)
    t = np.arange(CTX, dtype=np.float32)
    freqs = np.outer(t, inv_freq).astype(np.float32)
    emb = np.concatenate([freqs, freqs], axis=1)
    cos_t = np.cos(emb).T.astype(np.float32).copy()
    sin_t = np.sin(emb).T.astype(np.float32).copy()
    sin_s = sin_t.copy()
    sin_s[0:64] *= -1.0
    return cos_t, sin_s


def kernel(L0_attn_x, L0_attn_w_qkv, L0_attn_w_out, L0_ffn_w_gate, L0_ffn_w_up,
           L0_ffn_w_down, L0_ffn_s_gate, L0_ffn_s_up, L0_ffn_s_down):
    x = np.ascontiguousarray(np.asarray(L0_attn_x, dtype=np.float32))
    wqkv = np.asarray(L0_attn_w_qkv, dtype=np.float32)
    wq_t = np.ascontiguousarray(wqkv[0:DM, :].T)
    wkv_t = np.ascontiguousarray(wqkv[DM:3 * DM, :].T)
    wout_t = np.ascontiguousarray(np.asarray(L0_attn_w_out, dtype=np.float32).T)
    w_gate = np.ascontiguousarray(np.asarray(L0_ffn_w_gate, dtype=np.int8))
    w_up = np.ascontiguousarray(np.asarray(L0_ffn_w_up, dtype=np.int8))
    w_down = np.ascontiguousarray(np.asarray(L0_ffn_w_down, dtype=np.int8))
    sg = np.asarray(L0_ffn_s_gate, dtype=np.float32)
    su = np.asarray(L0_ffn_s_up, dtype=np.float32)
    sd = np.ascontiguousarray(np.asarray(L0_ffn_s_down, dtype=np.float32))
    sg_pt = np.ascontiguousarray(sg.reshape(INTER // 128, 128).T)
    su_pt = np.ascontiguousarray(su.reshape(INTER // 128, 128).T)
    cos_t, sin_s = _rope_tables()
    ident = np.eye(128, dtype=ml_dtypes.bfloat16)

    in_maps = []
    for core in range(8):
        b, g = core // 2, core % 2
        own = x[b, g * TOK:(g + 1) * TOK]
        oth = x[b, (1 - g) * TOK:(2 - g) * TOK]
        x_ctx = np.ascontiguousarray(np.concatenate([own, oth], axis=0))
        cq = np.ascontiguousarray(cos_t[:, g * TOK:(g + 1) * TOK])
        sq = np.ascontiguousarray(sin_s[:, g * TOK:(g + 1) * TOK])
        ck = np.ascontiguousarray(np.concatenate(
            [cos_t[:, g * TOK:(g + 1) * TOK], cos_t[:, (1 - g) * TOK:(2 - g) * TOK]], axis=1))
        sk = np.ascontiguousarray(np.concatenate(
            [sin_s[:, g * TOK:(g + 1) * TOK], sin_s[:, (1 - g) * TOK:(2 - g) * TOK]], axis=1))
        in_maps.append(dict(
            x_ctx=x_ctx, wq_t=wq_t, wkv_t=wkv_t, wout_t=wout_t,
            w_gate=w_gate, w_up=w_up, w_down=w_down,
            sg_pt=sg_pt, su_pt=su_pt, s_down=sd,
            cosq=cq, sinq=sq, cosk=ck, sink=sk, ident=ident,
        ))

    nc = _get_nc()
    res = run_bass_kernel_spmd(nc, in_maps, core_ids=list(range(8)))
    out = np.empty((B, CTX, DM), dtype=np.float32)
    for core in range(8):
        b, g = core // 2, core % 2
        out[b, g * TOK:(g + 1) * TOK] = np.asarray(res.results[core]["y_out"], dtype=np.float32)
    return out
